# version 06 (final)
# baseline (speedup 1.0000x reference)
import numpy as np
from contextlib import ExitStack

import concourse.bass as bass
import concourse.mybir as mybir
from concourse.bass_utils import run_bass_kernel_spmd

F32 = mybir.dt.float32
BF16 = mybir.dt.bfloat16
I32 = mybir.dt.int32
I8 = mybir.dt.int8
ALU = mybir.AluOpType
AF = mybir.ActivationFunctionType
AX = mybir.AxisListType.X

D = 2048
TOK = 2048
HID = 8192
NCH = 16
HEADS = 16
NSUB = 16
HS = 256
NSLICE = HID // HS
NHC = HID // 128
DS = 256
NDS = D // DS
LNEPS = 1e-5
ATEPS = 1e-6
RC = 128.5
NFFN = float(D * HID)


class Sem:
    def __init__(self, ctx, nc, name):
        self.h = ctx.enter_context(nc.semaphore(name))
        self.n = 0


class Emitter:
    ENGS = ("sync", "tensor", "vector", "scalar", "gpsimd")

    def __init__(self):
        self.q = {e: [] for e in self.ENGS}
        self.waits = []

    def em(self, eng, fn, waits=(), inc=None, inc_by=1):
        for w in waits:
            if w is None:
                continue
            sem, v = w
            if sem is None or not v:
                continue
            self.waits.append((sem, v))
            self.q[eng].append(lambda e, _s=sem, _v=v: e.wait_ge(_s.h, _v))
        if inc is not None:
            inc.n += inc_by
            self.q[eng].append(lambda e, _f=fn, _s=inc, _b=inc_by: _f(e).then_inc(_s.h, _b))
            return inc.n
        self.q[eng].append(fn)
        return None

    def dr(self, eng):
        self.q[eng].append(lambda e: e.drain())

    def check(self):
        for sem, v in self.waits:
            assert v <= sem.n, f"wait {v} > emitted total {sem.n}"


def build(debug=()):
    nc = bass.Bass("TRN2", target_bir_lowering=False, debug=False)

    x_h = nc.dram_tensor("x", [TOK, D], F32, kind="ExternalInput").ap()
    wqT = nc.dram_tensor("wqT", [D, D], F32, kind="ExternalInput").ap()
    wkT = nc.dram_tensor("wkT", [D, D], F32, kind="ExternalInput").ap()
    wvT = nc.dram_tensor("wvT", [D, D], F32, kind="ExternalInput").ap()
    woT = nc.dram_tensor("woT", [D, D], F32, kind="ExternalInput").ap()
    gwT = nc.dram_tensor("gwT", [D, HID], F32, kind="ExternalInput").ap()
    vwT = nc.dram_tensor("vwT", [D, HID], F32, kind="ExternalInput").ap()
    owT = nc.dram_tensor("owT", [HID, D], F32, kind="ExternalInput").ap()
    bia = {n: nc.dram_tensor(n, [D], F32, kind="ExternalInput").ap()
           for n in ("bq", "bk", "bv", "bo", "ob")}
    gb_h = nc.dram_tensor("gb", [HID], F32, kind="ExternalInput").ap()
    vb_h = nc.dram_tensor("vb", [HID], F32, kind="ExternalInput").ap()
    y_h = nc.dram_tensor("y", [TOK, D], F32, kind="ExternalOutput").ap()

    gc8 = nc.dram_tensor("gc8", [NSLICE, 128, NCH * HS], I8, kind="Internal").ap()
    vc8 = nc.dram_tensor("vc8", [NSLICE, 128, NCH * HS], I8, kind="Internal").ap()
    oc8 = nc.dram_tensor("oc8", [NHC, 128, D], I8, kind="Internal").ap()
    wqc = nc.dram_tensor("wqc", [NDS, 128, NCH * DS + DS], BF16, kind="Internal").ap()
    woc = nc.dram_tensor("woc", [NDS, 128, NCH * DS + DS], BF16, kind="Internal").ap()
    gbb = nc.dram_tensor("gbb", [1, HID], BF16, kind="Internal").ap()
    vbb = nc.dram_tensor("vbb", [1, HID], BF16, kind="Internal").ap()
    obb = nc.dram_tensor("obb", [1, D], BF16, kind="Internal").ap()
    bkb = nc.dram_tensor("bkb", [1, D], BF16, kind="Internal").ap()
    bvb = nc.dram_tensor("bvb", [1, D], BF16, kind="Internal").ap()

    dbg_done = set()
    E = Emitter()
    ctx = ExitStack()
    with ctx:
        LD = Sem(ctx, nc, "LD")
        GW = Sem(ctx, nc, "GW")
        LW = [Sem(ctx, nc, f"LW{i}") for i in range(2)]
        CQ = [Sem(ctx, nc, f"CQ{i}") for i in range(2)]
        SQ = [Sem(ctx, nc, f"SQ{i}") for i in range(2)]
        XL = [Sem(ctx, nc, f"XL{i}") for i in range(2)]
        WB = [Sem(ctx, nc, f"WB{i}") for i in range(2)]
        GG = [Sem(ctx, nc, f"GG{i}") for i in range(2)]
        GVS = [Sem(ctx, nc, f"GV{i}") for i in range(2)]
        GO = [Sem(ctx, nc, f"GO{i}") for i in range(2)]
        SBG = [Sem(ctx, nc, f"SBG{i}") for i in range(2)]
        SBV = [Sem(ctx, nc, f"SBV{i}") for i in range(2)]
        SY = [Sem(ctx, nc, f"SY{i}") for i in range(2)]
        XP = Sem(ctx, nc, "XP")
        TEN = Sem(ctx, nc, "TEN")
        VEC = Sem(ctx, nc, "VEC")
        SCA = Sem(ctx, nc, "SCA")
        GPC = Sem(ctx, nc, "GPC")
        ST = Sem(ctx, nc, "ST")

        def dbg_dump(name, src_ap, waits):
            if name in debug and name not in dbg_done:
                dbg_done.add(name)
                out = nc.dram_tensor("dbg_" + name, list(src_ap.shape), src_ap.dtype,
                                     kind="ExternalOutput").ap()
                v = E.em("sync", lambda e, _o=out, _s=src_ap: e.dma_start(out=_o, in_=_s),
                         waits=waits, inc=ST, inc_by=16)
                E.q["sync"].append(lambda e, _v=v: e.wait_ge(ST.h, _v))
                for eng in ("tensor", "vector", "scalar", "gpsimd"):
                    E.em(eng, lambda e: e.wait_ge(ST.h, v) if False else None)
                    E.q[eng].pop()
                    E.waits.append((ST, v))
                    E.q[eng].append(lambda e, _v=v: e.wait_ge(ST.h, _v))

        FA_N = 26656
        BA_N = 43008
        fa = ctx.enter_context(nc.sbuf_tensor("fa", [128, FA_N], F32))
        ba = ctx.enter_context(nc.sbuf_tensor("ba", [128, BA_N], BF16))
        s_t = ctx.enter_context(nc.sbuf_tensor("s_t", [128, 2048], I32))
        i8q = ctx.enter_context(nc.sbuf_tensor("i8q", [128, 4096], I8))
        ident_b = ctx.enter_context(nc.sbuf_tensor("ident_b", [128, 128], BF16))
        ident_f = ctx.enter_context(nc.sbuf_tensor("ident_f", [128, 128], F32))
        ones_b = ctx.enter_context(nc.sbuf_tensor("ones_b", [1, 128], BF16))
        decay_f = ctx.enter_context(nc.sbuf_tensor("decay_f", [128, 16], F32))
        decay_b = ctx.enter_context(nc.sbuf_tensor("decay_b", [128, 16], BF16))
        kv_b = ctx.enter_context(nc.sbuf_tensor("kv_b", [128, 2048], BF16))
        ksum_b = ctx.enter_context(nc.sbuf_tensor("ksum_b", [128, 16], BF16))
        gamw = ctx.enter_context(nc.sbuf_tensor("gamw", [128, 8], F32))
        stat = ctx.enter_context(nc.sbuf_tensor("stat", [128, 32], F32))
        zfb = ctx.enter_context(nc.sbuf_tensor("zfb", [128, 32], F32))
        rrow = ctx.enter_context(nc.sbuf_tensor("rrow", [1, 768], BF16))

        Pk = [ctx.enter_context(nc.psum_tensor(f"p{i}", [128, 512], F32)) for i in range(6)]
        PT = [ctx.enter_context(nc.psum_tensor(f"pt{i}", [128, 1024], BF16)) for i in range(2)]

        (S_SUMX, S_SUMSQ, S_MEAN, S_VAR, S_T, S_Y0, S_TMP, S_RSTD, S_NEGMR,
         S_GMAX, S_SX, S_CG0, S_CV0, S_CG1, S_CV1, S_HM0) = range(16)
        S_HM1, S_GH, S_SH, S_CO, S_NH = 16, 17, 18, 19, 20
        S_TMP2 = 23

        def sc(i):
            return stat[:, i:i + 1]

        def rr(i):
            return rrow[0:1, i * 128:(i + 1) * 128]

        E.em("gpsimd", lambda e: e.iota(iden_i[:], [[1, 128]], base=0, channel_multiplier=-1,
                                        allow_small_or_imprecise_dtypes=True))
        G_IOTA = E.em("gpsimd", lambda e: e.iota(decay_f[:], [[128, 16]], base=0,
                                                 channel_multiplier=1,
                                                 allow_small_or_imprecise_dtypes=True),
                      inc=GPC)
        E.em("vector", lambda e: e.tensor_single_scalar(ident_b[:], iden_i[:], 0.0,
                                                        ALU.is_equal), waits=[(GPC, G_IOTA)])
        E.em("vector", lambda e: e.tensor_single_scalar(ident_f[:], iden_i[:], 0.0,
                                                        ALU.is_equal))
        E.em("vector", lambda e: e.memset(ones_f[:], 1.0))
        V_CONST = E.em("vector", lambda e: e.memset(ones_b[:], 1.0), inc=VEC)
        SC_EXP = E.em("scalar", lambda e: e.activation(decay_f[:], decay_f[:], AF.Exp,
                                                       scale=-1.0 / TOK),
                      waits=[(GPC, G_IOTA)], inc=SCA)
        E.dr("scalar")
        dec_n = fa[:, 26624:26640]
        dec_ab = fa[:, 26640:26656]
        E.em("vector", lambda e: e.tensor_copy(
            dec_ab.bitcast(I32), decay_f[:].bitcast(I32)), waits=[(SCA, SC_EXP)])
        E.dr("vector")
        E.em("vector", lambda e: e.scalar_tensor_tensor(
            dec_n, decay_f[:], 0.0, dec_ab, ALU.mult, ALU.add))
        E.dr("vector")
        C_CONST = E.em("vector", lambda e: e.tensor_copy(decay_b[:], dec_n),
                       inc=VEC)
        E.dr("vector")
        dbg_dump("dec0", dec_n, [(VEC, C_CONST)])


        for src, dst in ((gb_h.unsqueeze(0), gbb), (vb_h.unsqueeze(0), vbb),
                         (bia["ob"].unsqueeze(0), obb), (bia["bk"].unsqueeze(0), bkb),
                         (bia["bv"].unsqueeze(0), bvb)):
            E.em("gpsimd", lambda e, _s=src, _d=dst: e.dma_start(out=_d, in_=_s),
                 inc=GW, inc_by=16)
        z_v = E.em("vector", lambda e: e.memset(ba[:, 0:256], 0.0), inc=VEC)
        for w_src, b_name, cache in ((wqT, "bq", wqc), (woT, "bo", woc)):
            for ds in range(NDS):
                src = w_src.rearrange("(g p) j -> p g j", p=128)[:, :, ds * DS:(ds + 1) * DS]
                E.em("gpsimd", lambda e, _s=src, _c=cache, _ds=ds: e.dma_start(
                    out=_c[_ds][:, 0:NCH * DS].rearrange("p (g j) -> p g j", j=DS),
                    in_=_s), inc=GW, inc_by=16)
                bsrc = bia[b_name][ds * DS:(ds + 1) * DS].unsqueeze(0)
                E.em("gpsimd", lambda e, _b=bsrc, _c=cache, _ds=ds: e.dma_start(
                    out=_c[_ds][0:1, NCH * DS:NCH * DS + DS], in_=_b),
                    inc=GW, inc_by=16)
                E.em("gpsimd", lambda e, _c=cache, _ds=ds: e.dma_start(
                    out=_c[_ds][1:128, NCH * DS:NCH * DS + DS],
                    in_=ba[1:128, 0:256]),
                    waits=[(VEC, z_v)] if (cache is wqc and ds == 0) else (),
                    inc=GW, inc_by=16)
        GW_CACHES = GW.n

        ones_f = fa[:, 24584:24712]
        iden_i = fa[:, 24712:24840]
        obrow = None
        wt = [fa[:, 0:8192], fa[:, 8192:16384]]
        colq = [fa[:, 16384:20480], fa[:, 20480:24576]]
        accg = fa[:, 24576:24580]
        psums = fa[:, 16384:16432]
        m1 = fa[:, 0:4096]
        m2 = fa[:, 4096:8192]

        V_ACC0 = E.em("vector", lambda e: e.memset(psums[:], 0.0), inc=VEC)
        gtiles = []
        for wi, w_ap in ((0, gwT), (1, vwT)):
            for g in range(NCH):
                gtiles.append((wi, w_ap[g * 128:(g + 1) * 128, :]))
        for hc in range(0, NHC, 4):
            gtiles.append((2, owT[hc * 128:(hc + 4) * 128, :]
                           .rearrange("(a p) j -> p a j", p=128)))
        prevv = [V_ACC0, V_ACC0]
        for i, (wi, src) in enumerate(gtiles):
            buf = wt[i % 2]
            v = E.em("sync", lambda e, _b=buf, _s=src: e.dma_start(
                out=_b.rearrange("p (a b) -> p a b", b=_s.shape[-1])
                if _s.ndim == 3 else _b, in_=_s),
                waits=[(VEC, prevv[i % 2])], inc=LW[i % 2], inc_by=16)
            prevv[i % 2] = E.em("vector", lambda e, _b=buf, _i=i: e.tensor_reduce(
                psums[:, _i:_i + 1], _b, AX, ALU.add, apply_absolute_value=True),
                waits=[(LW[i % 2], v)], inc=VEC)
        E.dr("vector")
        for wi in range(3):
            E.em("vector", lambda e, _wi=wi: e.tensor_reduce(
                accg[:, _wi:_wi + 1], psums[:, _wi * 16:(_wi + 1) * 16], AX, ALU.add),
                waits=[(VEC, max(prevv))] if wi == 0 else ())
        E.dr("vector")
        V_ACCD = E.em("vector", lambda e: e.tensor_copy(accg[:, 3:4], accg[:, 2:3]),
                      inc=VEC)
        T_GAM = E.em("tensor", lambda e: e.matmul(Pk[5][:, 0:4], ones_f[:], accg[:],
                                                  start=True, stop=True),
                     waits=[(VEC, V_ACCD)], inc=TEN)
        v_g0 = E.em("vector", lambda e: e.tensor_scalar(
            gamw[:, 0:3], Pk[5][:, 0:3], 1.0 / NFFN, 1e-5, ALU.mult, ALU.max),
            waits=[(TEN, T_GAM)], inc=VEC)
        E.em("gpsimd", lambda e: e.tensor_single_scalar(gamw[:, 4:7], gamw[:, 0:3],
                                                        0.5, ALU.mult),
             waits=[(VEC, v_g0)])
        E.dr("gpsimd")
        G_GAM = E.em("gpsimd", lambda e: e.tensor_single_scalar(
            stat[:, S_NH:S_NH + 3], gamw[:, 4:7], -1.0, ALU.mult), inc=GPC)
        V_GAM = E.em("vector", lambda e: e.tensor_copy(accg[:, 3:4], accg[:, 2:3]),
                     waits=[(GPC, G_GAM)], inc=VEC)
        dbg_dump("gamw", gamw[:, 0:3], [(VEC, V_GAM)])
        dbg_dump("accg", accg[:], [(VEC, max(prevv))])

        def qtiles():
            for sl in range(NSLICE):
                yield (0, gwT.rearrange("(g p) j -> p g j", p=128)[:, :, sl * HS:(sl + 1) * HS],
                       gc8[sl].rearrange("p (g j) -> p g j", j=HS))
            for sl in range(NSLICE):
                yield (1, vwT.rearrange("(g p) j -> p g j", p=128)[:, :, sl * HS:(sl + 1) * HS],
                       vc8[sl].rearrange("p (g j) -> p g j", j=HS))
            for hc in range(0, NHC, 2):
                yield (2, owT[hc * 128:(hc + 2) * 128, :].rearrange("(a p) j -> p a j", p=128),
                       oc8[hc:hc + 2].transpose([1, 0, 2]))

        qprev = [V_GAM, V_GAM]
        sprev = [0, 0]
        for i, (wi, src, dst) in enumerate(qtiles()):
            buf = colq[i % 2]
            v = E.em("sync", lambda e, _b=buf, _s=src: e.dma_start(
                out=_b.rearrange("p (a b) -> p a b", b=_s.shape[-1]) if _s.ndim == 3 else _b,
                in_=_s), waits=[(VEC, qprev[i % 2])], inc=CQ[i % 2], inc_by=16)
            E.em("vector", lambda e, _b=buf, _wi=wi: e.tensor_single_scalar(
                m1, _b, gamw[:, 4 + _wi:5 + _wi], ALU.is_ge), waits=[(CQ[i % 2], v)])
            E.dr("vector")
            E.em("vector", lambda e, _b=buf, _wi=wi: e.scalar_tensor_tensor(
                m2, _b, stat[:, S_NH + _wi:S_NH + _wi + 1], m1, ALU.is_ge, ALU.add))
            E.dr("vector")
            pp = (i + 1) % 2
            wex = [(SQ[pp], sprev[pp])] if sprev[pp] else []
            qprev[i % 2] = E.em("vector", lambda e: e.tensor_single_scalar(
                i8q[:], m2, 1.0, ALU.subtract), waits=wex, inc=VEC)
            sprev[i % 2] = E.em("sync", lambda e, _d=dst: e.dma_start(
                out=_d, in_=i8q[:].rearrange("p (a b) -> p a b", b=_d.shape[-1])
                if _d.ndim == 3 else i8q[:]), waits=[(VEC, qprev[i % 2])],
                inc=SQ[i % 2], inc_by=16)
        P0_SQ = [SQ[0].n, SQ[1].n]

        ln_dump = [fa[:, 4096:6144]]

        def emit_ln(xbuf, out_ap, vec_wait, sca_wait, norm_waits=()):
            v_red = E.em("vector", lambda e, _x=xbuf: e.reduce_sum(sc(S_SUMX), _x, AX),
                         waits=[vec_wait], inc=VEC)
            c_sq = E.em("scalar", lambda e, _x=xbuf, _d=ln_dump[0]: e.activation(
                _d, _x, AF.Square, accum_out=sc(S_SUMSQ)),
                waits=[sca_wait], inc=SCA)
            E.em("gpsimd", lambda e: e.tensor_single_scalar(sc(S_MEAN), sc(S_SUMX),
                                                            1.0 / D, ALU.mult),
                 waits=[(SCA, c_sq), (VEC, v_red)])
            E.em("gpsimd", lambda e: e.tensor_single_scalar(sc(S_TMP2), sc(S_SUMSQ),
                                                            1.0 / D, ALU.mult))
            E.dr("gpsimd")
            E.em("gpsimd", lambda e: e.tensor_tensor(sc(S_TMP), sc(S_MEAN),
                                                     sc(S_MEAN), ALU.mult))
            E.dr("gpsimd")
            E.em("gpsimd", lambda e: e.tensor_tensor(sc(S_VAR), sc(S_TMP2), sc(S_TMP),
                                                     ALU.subtract))
            E.dr("gpsimd")
            g_t = E.em("gpsimd", lambda e: e.tensor_single_scalar(
                sc(S_T), sc(S_VAR), LNEPS, ALU.add), inc=GPC)
            E.dr("gpsimd")
            E.em("scalar", lambda e: e.activation(sc(S_Y0), sc(S_T), AF.Ln),
                 waits=[(GPC, g_t)])
            E.dr("scalar")
            c_y0 = E.em("scalar", lambda e: e.activation(sc(S_Y0), sc(S_Y0), AF.Exp,
                                                         scale=-0.5), inc=SCA)
            E.em("gpsimd", lambda e: e.tensor_tensor(sc(S_TMP), sc(S_Y0), sc(S_Y0),
                                                     ALU.mult), waits=[(SCA, c_y0)])
            E.dr("gpsimd")
            E.em("gpsimd", lambda e: e.tensor_tensor(sc(S_TMP2), sc(S_T), sc(S_TMP),
                                                     ALU.mult))
            E.dr("gpsimd")
            E.em("gpsimd", lambda e: e.tensor_scalar(sc(S_TMP), sc(S_TMP2), -0.5,
                                                     1.5, ALU.mult, ALU.add))
            E.dr("gpsimd")
            E.em("gpsimd", lambda e: e.tensor_tensor(sc(S_RSTD), sc(S_Y0), sc(S_TMP),
                                                     ALU.mult))
            E.dr("gpsimd")
            E.em("gpsimd", lambda e: e.tensor_tensor(sc(S_TMP2), sc(S_MEAN),
                                                     sc(S_RSTD), ALU.mult))
            E.dr("gpsimd")
            g_r = E.em("gpsimd", lambda e: e.tensor_single_scalar(
                sc(S_NEGMR), sc(S_TMP2), -1.0, ALU.mult), inc=GPC)
            E.dr("vector")
            v_norm = E.em("vector", lambda e, _x=xbuf, _o=out_ap: e.tensor_scalar(
                _o, _x, sc(S_RSTD), sc(S_NEGMR), ALU.mult, ALU.add),
                waits=[(GPC, g_r)] + list(norm_waits), inc=VEC)
            return (VEC, v_norm)

        pt_free = [0, 0]

        def emit_transposes(src_tm, dst_slices, src_wait, dst_waits=()):
            last = 0
            for half in range(2):
                bank = half
                w = [src_wait] if half == 0 else []
                if pt_free[bank]:
                    w.append((SCA, pt_free[bank]))
                tv = None
                for k in range(8):
                    j = half * 8 + k
                    tv = E.em("tensor", lambda e, _j=j, _k=k, _b=bank: e.transpose(
                        PT[_b][:, _k * 128:(_k + 1) * 128],
                        src_tm[:, _j * 128:(_j + 1) * 128], ident_b[:]),
                        waits=w if k == 0 else (), inc=TEN if k == 7 else None)
                ew = [(TEN, tv)] + (list(dst_waits) if half == 0 else [])
                for k in range(8):
                    j = half * 8 + k
                    last = E.em("scalar", lambda e, _b=bank, _k=k, _d=dst_slices[j]:
                                e.copy(_d, PT[_b][:, _k * 128:(_k + 1) * 128]),
                                waits=ew if k == 0 else (),
                                inc=SCA if k == 7 else None)
                pt_free[bank] = last
            return last

        x_f = [fa[:, 0:2048], fa[:, 2048:4096]]
        kv_pp = [fa[:, 6144:8192], fa[:, 8192:10240]]
        ks_pp = [fa[:, 10240:10256], fa[:, 10256:10272]]
        wk_res = ba[:, 0:16384].rearrange("p (g j) -> p g j", j=1024)
        wv_res = ba[:, 16384:32768].rearrange("p (g j) -> p g j", j=1024)
        xn_tm1 = ba[:, 32768:34816]
        xn_fm1 = ba[:, 34816:36864]
        K_tm = ba[:, 36864:37888]
        V_tm = ba[:, 37888:38912]
        bkrow = ba[0:1, 38912:40960]
        bvrow = ba[0:1, 40960:43008]

        ones512 = fa[:, 10288:10800]
        decb = fa[:, 10800:11312]
        E.dr("vector")
        E.em("vector", lambda e: e.memset(ones512, 1.0), waits=[(VEC, V_GAM)])
        E.dr("vector")
        E.em("vector", lambda e: e.memset(kv_pp[0][:], 0.0))
        V_KV0 = E.em("vector", lambda e: e.memset(ks_pp[0][:], 0.0), inc=VEC)
        E.em("gpsimd", lambda e: e.dma_start(out=bkrow, in_=bkb),
             waits=[(GW, GW_CACHES)], inc=GW, inc_by=16)
        GW_ROWS = E.em("gpsimd", lambda e: e.dma_start(out=bvrow, in_=bvb),
                       inc=GW, inc_by=16)

        x_gate = [None, None]
        xn_tr_gate = 0
        kv_vec = V_KV0
        p4_gate = (None, 0)
        p5_gate = (VEC, V_GAM)
        xnfm_gate = 0

        for p in range(2):
            wkw = [(GW, GW_ROWS)] if p == 0 else [(TEN, TEN.n)]
            E.em("gpsimd", lambda e, _p=p: e.dma_start(
                out=wk_res, in_=wkT.rearrange("(g p) j -> p g j", p=128)
                [:, :, _p * 1024:(_p + 1) * 1024]), waits=wkw, inc=GW, inc_by=16)
            gw_w = E.em("gpsimd", lambda e, _p=p: e.dma_start(
                out=wv_res, in_=wvT.rearrange("(g p) j -> p g j", p=128)
                [:, :, _p * 1024:(_p + 1) * 1024]), inc=GW, inc_by=16)
            for u in range(NSUB):
                E.dr("vector")
                E.em("vector", lambda e, _u=u: e.tensor_scalar(
                    decb, ones512, dec_n[:, _u:_u + 1], None, ALU.mult),
                    waits=[(VEC, C_CONST)] if (p == 0 and u == 0) else ())
                E.dr("vector")
                xb = x_f[u % 2]
                xw = [x_gate[u % 2]]
                if p == 0 and u == 0:
                    xw.append((SQ[0], P0_SQ[0]))
                    xw.append((SQ[1], P0_SQ[1]))
                xv = E.em("sync", lambda e, _b=xb, _u=u: e.dma_start(
                    out=_b, in_=x_h[_u * 128:(_u + 1) * 128, :]), waits=xw,
                    inc=XL[u % 2], inc_by=16)
                w_norm = emit_ln(xb, xn_tm1, (XL[u % 2], xv), (XL[u % 2], xv),
                                 norm_waits=[(TEN, xn_tr_gate)])
                x_gate[u % 2] = w_norm
                c_xnfm = emit_transposes(
                    xn_tm1, [xn_fm1[:, j * 128:(j + 1) * 128] for j in range(16)],
                    w_norm, dst_waits=[(TEN, xnfm_gate)])
                xn_tr_gate = TEN.n
                if p == 0 and u == 0:
                    dbg_dump("xnf0", xn_fm1[:], [(SCA, c_xnfm)])
                for proj in range(2):
                    wres = wk_res if proj == 0 else wv_res
                    brow = bkrow if proj == 0 else bvrow
                    dst = K_tm if proj == 0 else V_tm
                    for ds in range(2):
                        bank = Pk[4 + ds]
                        w = [(SCA, c_xnfm), (GW, gw_w), p4_gate if ds == 0 else p5_gate]
                        E.em("tensor", lambda e, _b=bank, _r=brow, _p=p, _ds=ds: e.matmul(
                            _b[:, 0:512], ones_b[:],
                            _r[0:1, _p * 1024 + _ds * 512:_p * 1024 + (_ds + 1) * 512],
                            start=True, stop=False), waits=w)
                        tv = None
                        for g in range(NCH):
                            tv = E.em("tensor", lambda e, _b=bank, _w=wres, _g=g, _ds=ds:
                                      e.matmul(_b[:, 0:512],
                                               xn_fm1[:, _g * 128:(_g + 1) * 128],
                                               _w[:, _g, _ds * 512:(_ds + 1) * 512],
                                               start=False, stop=(_g == NCH - 1)),
                                      inc=TEN if g == NCH - 1 else None)
                        if proj == 0:
                            cv = E.em("scalar", lambda e, _b=bank, _ds=ds, _d=dst:
                                      e.activation(_d[:, _ds * 512:(_ds + 1) * 512], _b[:],
                                                   AF.Relu), waits=[(TEN, tv)], inc=SCA)
                        else:
                            E.dr("vector")
                            cvv = E.em("vector", lambda e, _b=bank, _ds=ds, _d=dst:
                                       e.tensor_tensor(_d[:, _ds * 512:(_ds + 1) * 512],
                                                       _b[:], decb, ALU.mult),
                                       waits=[(TEN, tv)], inc=VEC)
                            cv = E.em("scalar", lambda e: e.drain(),
                                      waits=[(VEC, cvv)], inc=SCA)
                        if ds == 0:
                            p4_gate = (SCA, cv)
                        else:
                            p5_gate = (SCA, cv)
                c_kvready = SCA.n
                for hl in range(8):
                    w = [(SCA, c_kvready), (VEC, kv_vec)] if hl == 0 else []
                    E.em("tensor", lambda e, _hl=hl: e.matmul(
                        Pk[_hl // 4][:, (_hl % 4) * 128:(_hl % 4 + 1) * 128],
                        K_tm[:, _hl * 128:(_hl + 1) * 128],
                        V_tm[:, _hl * 128:(_hl + 1) * 128],
                        start=True, stop=True), waits=w)
                tv = None
                for hl in range(8):
                    tv = E.em("tensor", lambda e, _hl=hl, _u=u: e.matmul(
                        Pk[4][:, _hl:_hl + 1], K_tm[:, _hl * 128:(_hl + 1) * 128],
                        decay_b[:, _u:_u + 1], start=True, stop=True),
                        waits=[p4_gate, (VEC, C_CONST)] if hl == 0 else (),
                        inc=TEN if hl == 7 else None)
                sb_, db_ = kv_pp[u % 2], kv_pp[(u + 1) % 2]
                E.dr("vector")
                E.em("vector", lambda e, _p=p, _s=sb_, _d=db_: e.tensor_tensor(
                    _d[:, _p * 1024:_p * 1024 + 512],
                    _s[:, _p * 1024:_p * 1024 + 512], Pk[0][:], ALU.add),
                    waits=[(TEN, tv)])
                E.em("vector", lambda e, _p=p, _s=sb_, _d=db_: e.tensor_tensor(
                    _d[:, _p * 1024 + 512:(_p + 1) * 1024],
                    _s[:, _p * 1024 + 512:(_p + 1) * 1024], Pk[1][:], ALU.add))
                kv_vec = E.em("vector", lambda e, _p=p, _u=u: e.tensor_tensor(
                    ks_pp[(_u + 1) % 2][:, _p * 8:(_p + 1) * 8],
                    ks_pp[_u % 2][:, _p * 8:(_p + 1) * 8],
                    Pk[4][:, 0:8], ALU.add), inc=VEC)
                p4_gate = (VEC, kv_vec)
                xnfm_gate = TEN.n
                if p == 0 and u == 0:
                    dbg_dump("xload0", xb[:], [(XL[u % 2], xv)])
                    dbg_dump("stat0", stat[:, 0:16], [w_norm])
                    dbg_dump("xn0", xn_tm1[:], [w_norm])
                    dbg_dump("ktm0", K_tm[:], [(SCA, c_kvready)])
                    dbg_dump("vtm0", V_tm[:], [(SCA, c_kvready)])

        E.dr("vector")
        E.em("vector", lambda e: e.tensor_copy(kv_b[:], kv_pp[0][:]), waits=[(VEC, kv_vec)])
        V_KVB = E.em("vector", lambda e: e.tensor_copy(ksum_b[:], ks_pp[0][:]), inc=VEC)
        dbg_dump("kv", kv_pp[0][:], [(VEC, V_KVB)])
        dbg_dump("ksum", ks_pp[0][:], [(VEC, V_KVB)])

        hid_f = [fa[:, 0:8192], fa[:, 8192:16384]]
        x2_f = [fa[:, 16384:18432], fa[:, 18432:20480]]
        xP2 = fa[:, 20480:22528]
        xn2 = fa[:, 22528:24576]
        s_w = fa[:, 24576:26624]
        ln_dump[0] = xn2

        wbuf = [ba[:, 0:4352], ba[:, 4352:8704]]
        gbuf = [ba[:, 8704:13056], ba[:, 13056:17408]]
        vbuf = [ba[:, 17408:21760], ba[:, 21760:26112]]
        obuf = [ba[:, 8704:12800], ba[:, 13056:17152]]
        slotA = ba[:, 26112:28160]
        slotB = ba[:, 28160:30208]
        Q_tm = ba[:, 30208:32256]
        slotC = ba[:, 32256:34304]
        xdq_fm = ba[:, 34304:38400].rearrange("p (g j) -> p g j", j=256)

        obrow = ba[0:1, 38912:40960]
        T_P1DONE = TEN.n
        GW_OBROW = E.em("gpsimd", lambda e: e.dma_start(out=obrow, in_=obb),
                        waits=[(TEN, T_P1DONE)], inc=GW, inc_by=16)
        wbuf_gate = [(TEN, T_P1DONE), (TEN, T_P1DONE)]
        gw_gate = [[(TEN, T_P1DONE), (TEN, T_P1DONE)],
                   [(TEN, T_P1DONE), (TEN, T_P1DONE)]]
        ob_gate = [(None, 0), (None, 0)]
        y_store = 0
        x_gate2 = (None, 0)
        att_gate = 0
        pq_gate = [(None, 0), (None, 0)]
        slotA_gate = 0
        slotB_gate = 0
        slotC_gate = 0
        qtm_gate = 0
        xdq_gate = 0

        def fill_wq(kind, ds, bi):
            cache = wqc if kind == 0 else woc
            return E.em("sync", lambda e, _c=cache, _ds=ds, _b=wbuf[bi]: e.dma_start(
                out=_b, in_=_c[_ds]), waits=[wbuf_gate[bi]], inc=WB[bi], inc_by=16)

        for t in range(8):
            for s in range(2):
                u = 2 * t + s
                xv = E.em("sync", lambda e, _u=u: e.dma_start(
                    out=xP2, in_=x_h[_u * 128:(_u + 1) * 128, :]),
                    waits=[x_gate2, (SY[0], SY[0].n), (SY[1], SY[1].n),
                           (SCA, SCA.n)], inc=XP, inc_by=16)
                ln_dump[0] = xn2
                w_norm = emit_ln(xP2, slotA, (XP, xv), (XP, xv),
                                 norm_waits=[(TEN, slotA_gate)])
                c_xnfm = emit_transposes(
                    slotA, [slotB[:, j * 128:(j + 1) * 128] for j in range(16)],
                    w_norm, dst_waits=[(TEN, slotB_gate)])
                slotA_gate_local = TEN.n
                for ds in range(NDS):
                    bi = ds % 2
                    fv = fill_wq(0, ds, bi)
                    bank = Pk[4 + bi]
                    w = [(WB[bi], fv), (SCA, c_xnfm), (TEN, qtm_gate), pq_gate[bi]]
                    E.em("tensor", lambda e, _b=bank, _bi=bi: e.matmul(
                        _b[:, 0:DS], ones_b[:], wbuf[_bi][0:1, NCH * DS:NCH * DS + DS],
                        start=True, stop=False), waits=w)
                    tv = None
                    for g in range(NCH):
                        tv = E.em("tensor", lambda e, _b=bank, _bi=bi, _g=g: e.matmul(
                            _b[:, 0:DS], slotB[:, _g * 128:(_g + 1) * 128],
                            wbuf[_bi][:, 0:NCH * DS].rearrange(
                                "p (g j) -> p g j", j=DS)[:, _g, :],
                            start=False, stop=(_g == NCH - 1)),
                            inc=TEN if g == NCH - 1 else None)
                    wbuf_gate[bi] = (TEN, tv)
                    cv = E.em("scalar", lambda e, _b=bank, _ds=ds: e.activation(
                        Q_tm[:, _ds * DS:(_ds + 1) * DS], _b[:, 0:DS], AF.Relu),
                        waits=[(TEN, tv)], inc=SCA)
                    pq_gate[bi] = (SCA, cv)
                c_qtm = SCA.n
                slotB_gate = TEN.n
                c_qfm = emit_transposes(
                    Q_tm, [slotC[:, j * 128:(j + 1) * 128] for j in range(16)],
                    (SCA, c_qtm), dst_waits=[(TEN, slotC_gate)])
                qtm_gate = TEN.n
                for h in range(HEADS):
                    w = []
                    if h == 0:
                        w = [(SCA, c_qfm), (VEC, V_KVB), (VEC, att_gate)]
                    E.em("tensor", lambda e, _h=h: e.matmul(
                        Pk[_h // 4][:, (_h % 4) * 128:(_h % 4 + 1) * 128],
                        slotC[:, _h * 128:(_h + 1) * 128],
                        kv_b[:, _h * 128:(_h + 1) * 128],
                        start=True, stop=True), waits=w)
                t_att = None
                for h in range(HEADS):
                    t_att = E.em("tensor", lambda e, _h=h: e.matmul(
                        Pk[5][:, _h:_h + 1], slotC[:, _h * 128:(_h + 1) * 128],
                        ksum_b[:, _h:_h + 1], start=True, stop=True),
                        waits=[pq_gate[1]] if h == 0 else (),
                        inc=TEN if h == HEADS - 1 else None)
                slotC_gate = TEN.n
                E.em("vector", lambda e, _u=u: e.tensor_scalar(
                    zfb[:, 0:16], Pk[5][:, 0:16], decay_f[:, _u:_u + 1], ATEPS,
                    ALU.mult, ALU.add), waits=[(TEN, t_att)])
                E.dr("vector")
                E.em("vector", lambda e: e.reciprocal(zfb[:, 16:32], zfb[:, 0:16]))
                E.dr("vector")
                v_zf = E.em("vector", lambda e, _u=u: e.tensor_scalar(
                    zfb[:, 0:16], zfb[:, 16:32], decay_f[:, _u:_u + 1], None,
                    ALU.mult), inc=VEC)
                E.dr("vector")
                pq_gate[1] = (VEC, v_zf)
                lastv = None
                for h in range(HEADS):
                    w = [(TEN, slotA_gate_local)] if h == 0 else ()
                    lastv = E.em("vector", lambda e, _h=h: e.tensor_scalar(
                        slotA[:, _h * 128:(_h + 1) * 128],
                        Pk[_h // 4][:, (_h % 4) * 128:(_h % 4 + 1) * 128],
                        zfb[:, _h:_h + 1], None, ALU.mult), waits=w,
                        inc=VEC if h == HEADS - 1 else None)
                att_gate = lastv
                if t == 0 and s == 0:
                    dbg_dump("att0", slotA[:], [(VEC, lastv)])
                c_aofm = emit_transposes(
                    slotA, [slotB[:, j * 128:(j + 1) * 128] for j in range(16)],
                    (VEC, lastv), dst_waits=[(TEN, slotB_gate)])
                slotA_gate = TEN.n
                vv = None
                for ds in range(NDS):
                    bi = ds % 2
                    fv = fill_wq(1, ds, bi)
                    bank = Pk[4 + bi]
                    w = [(WB[bi], fv), (SCA, c_aofm), pq_gate[bi]]
                    E.em("tensor", lambda e, _b=bank, _bi=bi: e.matmul(
                        _b[:, 0:DS], ones_b[:], wbuf[_bi][0:1, NCH * DS:NCH * DS + DS],
                        start=True, stop=False), waits=w)
                    tv = None
                    for g in range(NCH):
                        tv = E.em("tensor", lambda e, _b=bank, _bi=bi, _g=g: e.matmul(
                            _b[:, 0:DS], slotB[:, _g * 128:(_g + 1) * 128],
                            wbuf[_bi][:, 0:NCH * DS].rearrange(
                                "p (g j) -> p g j", j=DS)[:, _g, :],
                            start=False, stop=(_g == NCH - 1)),
                            inc=TEN if g == NCH - 1 else None)
                    wbuf_gate[bi] = (TEN, tv)
                    w2 = [(TEN, tv)]
                    if ds == 0:
                        w2.append(x_gate2)
                    vv = E.em("vector", lambda e, _b=bank, _ds=ds, _s=s:
                              e.scalar_tensor_tensor(
                                  x2_f[_s][:, _ds * DS:(_ds + 1) * DS], _b[:, 0:DS], 1.0,
                                  xP2[:, _ds * DS:(_ds + 1) * DS], ALU.mult, ALU.add),
                              waits=w2, inc=VEC)
                    pq_gate[bi] = (VEC, vv)
                x_gate2 = (VEC, vv)
                slotB_gate = TEN.n
                ln_dump[0] = s_w
                w_ln2 = emit_ln(x2_f[s], xn2, (VEC, vv), (VEC, vv))
                if t == 0 and s == 0:
                    dbg_dump("xn20", xn2[:], [w_ln2])
                E.dr("vector")
                E.em("vector", lambda e: e.reduce_max(
                    sc(S_TMP), xn2, AX, apply_absolute_value=True), waits=[w_ln2])
                E.dr("vector")
                v_gm = E.em("vector", lambda e: e.tensor_single_scalar(
                    sc(S_GMAX), sc(S_TMP), 1e-5, ALU.max), inc=VEC)
                cg_i = S_CG0 if s == 0 else S_CG1
                cv_i = S_CV0 if s == 0 else S_CV1
                E.em("gpsimd", lambda e: e.tensor_tensor(
                    sc(S_TMP2), sc(S_GMAX), gamw[:, 0:1], ALU.mult),
                    waits=[(VEC, v_gm)])
                E.dr("gpsimd")
                E.em("gpsimd", lambda e, _i=cg_i: e.tensor_single_scalar(
                    sc(_i), sc(S_TMP2), 1.0 / 127.0, ALU.mult))
                E.dr("gpsimd")
                E.em("gpsimd", lambda e: e.tensor_tensor(
                    sc(S_TMP2), sc(S_GMAX), gamw[:, 1:2], ALU.mult))
                E.dr("gpsimd")
                g_cv = E.em("gpsimd", lambda e, _i=cv_i: e.tensor_single_scalar(
                    sc(_i), sc(S_TMP2), 1.0 / 127.0, ALU.mult), inc=GPC)
                E.dr("gpsimd")
                E.em("vector", lambda e: e.reciprocal(sc(S_TMP), sc(S_GMAX)),
                     waits=[(GPC, g_cv)])
                E.em("vector", lambda e, _i=cg_i: e.reciprocal(sc(S_TMP2), sc(_i)))
                v_rc0 = E.em("vector", lambda e, _i=cv_i: e.reciprocal(sc(S_VAR), sc(_i)),
                             inc=VEC)
                g_sx = E.em("gpsimd", lambda e: e.tensor_single_scalar(
                    sc(S_SX), sc(S_TMP), 127.0, ALU.mult),
                    waits=[(VEC, v_rc0)], inc=GPC)
                v_rc = v_rc0
                t_r = None
                t_r = E.em("tensor", lambda e: e.transpose(
                    Pk[5][0:1, 0:128], sc(S_TMP2), ident_f[:]),
                    waits=[(VEC, v_rc), pq_gate[1]], inc=TEN)
                t_r = E.em("tensor", lambda e: e.transpose(
                    Pk[5][0:1, 128:256], sc(S_VAR), ident_f[:]), inc=TEN)
                c_rows = E.em("scalar", lambda e, _s=s: e.copy(rr(2 * _s), Pk[5][0:1, 0:128]),
                              waits=[(TEN, t_r)], inc=SCA)
                c_rows = E.em("scalar", lambda e, _s=s: e.copy(rr(2 * _s + 1),
                                                               Pk[5][0:1, 128:256]), inc=SCA)
                pq_gate[1] = (SCA, c_rows)
                E.em("vector", lambda e: e.tensor_scalar(
                    s_w, xn2, sc(S_SX), RC, ALU.mult, ALU.add),
                    waits=[(GPC, g_sx)])
                E.dr("vector")
                E.em("vector", lambda e: e.tensor_copy(s_t[:], s_w))
                E.dr("vector")
                E.em("vector", lambda e: e.scalar_tensor_tensor(
                    xn2, s_t[:], 0.0, s_w, ALU.add, ALU.is_gt))
                E.dr("vector")
                E.em("vector", lambda e: e.scalar_tensor_tensor(
                    slotB, s_t[:], 1.0, s_w, ALU.add, ALU.is_le),
                    waits=[(TEN, slotB_gate)])
                E.dr("vector")
                E.em("vector", lambda e: e.tensor_tensor(
                    s_w, xn2, slotB, ALU.subtract))
                E.dr("vector")
                v_R = E.em("vector", lambda e: e.scalar_tensor_tensor(
                    slotA, s_t[:], -128.0, s_w, ALU.add, ALU.subtract),
                    waits=[(TEN, slotA_gate)], inc=VEC)
                if t == 0 and s == 0:
                    dbg_dump("rx0", slotA[:], [(VEC, v_R)])
                    dbg_dump("sc0", stat[:, 9:13], [(VEC, v_R)])
                c_xdq = emit_transposes(
                    slotA, [xdq_fm[:, j, s * 128:(s + 1) * 128] for j in range(16)],
                    (VEC, v_R), dst_waits=[(TEN, xdq_gate)] if s == 0 else ())
                slotA_gate = TEN.n
                if t == 0 and s == 0:
                    dbg_dump("qtm0", Q_tm[:], [(SCA, c_qtm)])
                    dbg_dump("x20", x2_f[0][:], [(VEC, vv)])
                    dbg_dump("zf0", zfb[:], [(VEC, v_zf)])
            if t == 0:
                dbg_dump("xdq0", ba[:, 34304:38400], [(SCA, SCA.n)])
            HMC = [[24, 25], [26, 27]]
            E.em("vector", lambda e: e.memset(stat[:, 24:25], 0.0))
            v_hm0 = E.em("vector", lambda e: e.memset(stat[:, 26:27], 0.0),
                         inc=VEC)
            gv_psum_gate = {4: (None, 0), 5: pq_gate[1], 2: (None, 0), 3: (None, 0)}
            g_2a = GPC.n
            gv_psum_gate[2] = (VEC, att_gate)
            gv_psum_gate[3] = (VEC, att_gate)
            gv_psum_gate[4] = pq_gate[0]
            last_silu = 0
            last_val = 0
            h_done = [0, 0]
            c_2a_done = SCA.n
            for sl in range(NSLICE):
                bi = sl % 2
                gwv = E.em("gpsimd", lambda e, _sl=sl, _bi=bi: e.dma_start(
                    out=gbuf[_bi][:, 0:NCH * HS], in_=gc8[_sl]),
                    waits=[gw_gate[0][bi], ob_gate[bi] if sl < 2 else None],
                    inc=GG[bi], inc_by=16)
                gwv2 = E.em("gpsimd", lambda e, _sl=sl, _bi=bi: e.dma_start(
                    out=vbuf[_bi][:, 0:NCH * HS], in_=vc8[_sl]),
                    waits=[gw_gate[1][bi]], inc=GVS[bi], inc_by=16)
                bv1 = E.em("sync", lambda e, _sl=sl, _bi=bi: e.dma_start(
                    out=gbuf[_bi][0:1, NCH * HS:NCH * HS + HS],
                    in_=gbb[0:1, _sl * HS:(_sl + 1) * HS]),
                    waits=[gw_gate[0][bi]], inc=SBG[bi], inc_by=16)
                bv2 = E.em("sync", lambda e, _sl=sl, _bi=bi: e.dma_start(
                    out=vbuf[_bi][0:1, NCH * HS:NCH * HS + HS],
                    in_=vbb[0:1, _sl * HS:(_sl + 1) * HS]),
                    waits=[gw_gate[1][bi]], inc=SBV[bi], inc_by=16)
                for s in range(2):
                    gb_bank = 4 + s
                    vb_bank = 2 + s
                    for which, bank_i, buf, brow_i, fillw in (
                            (0, gb_bank, gbuf, 2 * s, (GG[bi], gwv)),
                            (1, vb_bank, vbuf, 2 * s + 1, (GVS[bi], gwv2))):
                        bank = Pk[bank_i]
                        w = [fillw, (SBG[bi] if which == 0 else SBV[bi],
                                     bv1 if which == 0 else bv2),
                             gv_psum_gate[bank_i]]
                        if sl == 0 and s == 0:
                            w.append((SCA, c_2a_done))
                        E.em("tensor", lambda e, _b=bank, _bi=bi, _ri=brow_i, _buf=buf:
                             e.matmul(_b[:, 0:HS], rr(_ri),
                                      _buf[_bi][0:1, NCH * HS:NCH * HS + HS],
                                      start=True, stop=False), waits=w)
                        tv = None
                        for g in range(NCH):
                            tv = E.em("tensor", lambda e, _b=bank, _bi=bi, _g=g, _s=s,
                                      _buf=buf: e.matmul(
                                          _b[:, 0:HS], xdq_fm[:, _g, _s * 128:(_s + 1) * 128],
                                          _buf[_bi][:, _g * HS:(_g + 1) * HS],
                                          start=False, stop=(_g == NCH - 1)),
                                      inc=TEN if g == NCH - 1 else None)
                        if which == 0:
                            gw_gate[0][bi] = (TEN, tv)
                        else:
                            gw_gate[1][bi] = (TEN, tv)
                        if which == 0:
                            last_silu = E.em("scalar", lambda e, _b=bank, _s=s: e.activation(
                                s_w[:, _s * 1024:_s * 1024 + HS], _b[:, 0:HS], AF.Silu,
                                scale=sc(S_CG0 if _s == 0 else S_CG1)),
                                waits=[(TEN, tv), (VEC, h_done[s])]
                                + ([(GPC, g_2a)] if (sl == 0 and s == 0) else []),
                                inc=SCA)
                            gv_psum_gate[bank_i] = (SCA, last_silu)
                        else:
                            last_val = E.em("vector", lambda e, _b=bank, _s=s: e.tensor_scalar(
                                s_w[:, _s * 1024 + HS:_s * 1024 + 2 * HS], _b[:, 0:HS],
                                sc(S_CV0 if _s == 0 else S_CV1), None, ALU.mult),
                                waits=[(TEN, tv)], inc=VEC)
                            gv_psum_gate[bank_i] = (VEC, last_val)
                    E.dr("vector")
                    E.em("vector", lambda e, _s=s, _sl=sl: e.tensor_tensor(
                        hid_f[_s][:, _sl * HS:(_sl + 1) * HS],
                        s_w[:, _s * 1024:_s * 1024 + HS],
                        s_w[:, _s * 1024 + HS:_s * 1024 + 2 * HS], ALU.mult),
                        waits=[(SCA, last_silu)])
                    E.dr("vector")
                    E.em("vector", lambda e, _s=s, _sl=sl: e.reduce_max(
                        stat[:, 28 + _s:29 + _s],
                        hid_f[_s][:, _sl * HS:(_sl + 1) * HS], AX,
                        apply_absolute_value=True))
                    E.dr("vector")
                    h_done[s] = E.em("vector", lambda e, _s=s, _sl=sl: e.tensor_tensor(
                        sc(HMC[_s][(_sl + 1) % 2]), sc(HMC[_s][_sl % 2]),
                        stat[:, 28 + _s:29 + _s], ALU.max), inc=VEC)
            V_2B = E.em("vector", lambda e: e.tensor_copy(sc(S_TMP2), sc(S_TMP)), inc=VEC)
            xdq_gate = TEN.n
            if t == 0:
                dbg_dump("hid0", hid_f[0][:], [(VEC, V_2B)])
            for s in range(2):
                E.dr("vector")
                v_gh = E.em("vector", lambda e, _s=s: e.tensor_single_scalar(
                    sc(S_GH), sc(HMC[_s][0]), 1e-5, ALU.max),
                    waits=[(VEC, V_2B)], inc=VEC)
                E.em("gpsimd", lambda e: e.tensor_tensor(
                    sc(S_TMP2), sc(S_GH), gamw[:, 2:3], ALU.mult),
                     waits=[(VEC, v_gh)])
                E.dr("gpsimd")
                g_co = E.em("gpsimd", lambda e: e.tensor_single_scalar(
                    sc(S_CO), sc(S_TMP2), 1.0 / 127.0, ALU.mult), inc=GPC)
                E.dr("vector")
                E.em("vector", lambda e: e.reciprocal(sc(S_TMP), sc(S_GH)),
                     waits=[(GPC, g_co)])
                v_ro = E.em("vector", lambda e: e.reciprocal(sc(S_VAR), sc(S_CO)),
                            inc=VEC)
                g_sh = E.em("gpsimd", lambda e: e.tensor_single_scalar(
                    sc(S_SH), sc(S_TMP), 127.0, ALU.mult),
                    waits=[(VEC, v_ro)], inc=GPC)
                t_ro = E.em("tensor", lambda e: e.transpose(
                    Pk[5][0:1, 256:384], sc(S_VAR), ident_f[:]),
                    waits=[(VEC, v_ro), gv_psum_gate[5]], inc=TEN)
                c_ro = E.em("scalar", lambda e, _s=s: e.copy(rr(4 + _s),
                                                             Pk[5][0:1, 256:384]),
                            waits=[(TEN, t_ro)], inc=SCA)
                gv_psum_gate[5] = (SCA, c_ro)
                for d4 in range(4):
                    w = [(SCA, c_ro), (GW, GW_OBROW)]
                    if d4 == 0:
                        w.append((VEC, att_gate) if (t == 0 and s == 0) else
                                 (VEC, att_gate))
                    E.em("tensor", lambda e, _d=d4, _s=s: e.matmul(
                        Pk[_d][:, 0:512], rr(4 + _s),
                        obrow[0:1, _d * 512:(_d + 1) * 512],
                        start=True, stop=False), waits=w)
                for q in range(4):
                    E.dr("vector")
                    E.em("vector", lambda e, _s=s, _q=q: e.tensor_scalar(
                        s_w, hid_f[_s][:, _q * 2048:(_q + 1) * 2048], sc(S_SH), RC,
                        ALU.mult, ALU.add), waits=[(GPC, g_sh)] if q == 0 else ())
                    E.dr("vector")
                    E.em("vector", lambda e: e.tensor_copy(s_t[:], s_w))
                    E.dr("vector")
                    E.em("vector", lambda e: e.scalar_tensor_tensor(
                        xn2, s_t[:], 0.0, s_w, ALU.add, ALU.is_gt))
                    E.dr("vector")
                    E.em("vector", lambda e: e.scalar_tensor_tensor(
                        slotB, s_t[:], 1.0, s_w, ALU.add, ALU.is_le),
                        waits=[(TEN, slotB_gate)] if (s == 0 and q == 0) else ())
                    E.dr("vector")
                    E.em("vector", lambda e: e.tensor_tensor(
                        s_w, xn2, slotB, ALU.subtract))
                    E.dr("vector")
                    v_Rh = E.em("vector", lambda e: e.scalar_tensor_tensor(
                        slotA, s_t[:], -128.0, s_w, ALU.add, ALU.subtract),
                        waits=[(TEN, slotA_gate)], inc=VEC)
                    if t == 0 and s == 0 and q == 0:
                        dbg_dump("rh0", slotA[:], [(VEC, v_Rh)])
                        dbg_dump("shco", stat[:, 0:32], [(VEC, v_Rh)])
                    c_rh = emit_transposes(
                        slotA, [slotC[:, j * 128:(j + 1) * 128] for j in range(16)],
                        (VEC, v_Rh), dst_waits=[(TEN, slotC_gate)])
                    slotA_gate = TEN.n
                    if t == 0 and s == 0 and q == 0:
                        dbg_dump("rhfm0", slotC[:], [(SCA, c_rh)])
                    for hcl in range(16):
                        hc = q * 16 + hcl
                        if hc % 2 == 0:
                            obi = (hc // 2) % 2
                            ogv = E.em("gpsimd", lambda e, _hc=hc, _o=obi: e.dma_start(
                                out=obuf[_o][:, 0:2 * D].rearrange("p (a j) -> p a j", j=D),
                                in_=oc8[_hc:_hc + 2].transpose([1, 0, 2])),
                                waits=[ob_gate[obi],
                                       gw_gate[0][obi] if (s == 0 and q == 0 and hc < 4)
                                       else None,
                                       gw_gate[1][obi] if (s == 0 and q == 0 and hc < 4)
                                       else None],
                                inc=GO[obi], inc_by=16)
                            ob_ld = ogv
                            ob_sem = GO[obi]
                        tv = None
                        for d4 in range(4):
                            w = [(SCA, c_rh), (ob_sem, ob_ld)] if d4 == 0 else []
                            tv = E.em("tensor", lambda e, _d=d4, _hcl=hcl, _hc=hc:
                                      e.matmul(Pk[_d][:, 0:512],
                                               slotC[:, _hcl * 128:(_hcl + 1) * 128],
                                               obuf[(_hc // 2) % 2]
                                               [:, (_hc % 2) * D + _d * 512:
                                                (_hc % 2) * D + (_d + 1) * 512],
                                               start=False,
                                               stop=(_hc == NHC - 1)),
                                      waits=w, inc=TEN if d4 == 3 else None)
                        if t == 0 and s == 0 and hc == 0 and "ps00" in debug:
                            c_pd = E.em("scalar", lambda e: e.copy(xn2[:, 0:512],
                                                                   Pk[0][:]),
                                        waits=[(TEN, tv)], inc=SCA)
                            dbg_dump("ps00", xn2[:, 0:512], [(SCA, c_pd)])
                        if hc % 2 == 1:
                            ob_gate[(hc // 2) % 2] = (TEN, tv)
                    slotC_gate = TEN.n
                t_out = TEN.n
                ybuf = xP2 if s == 0 else xn2
                E.dr("vector")
                lastv2 = None
                for d4 in range(4):
                    w = [(TEN, t_out)] if d4 == 0 else []
                    if d4 == 0 and s == 0:
                        w.append(x_gate2)
                    lastv2 = E.em("vector", lambda e, _d=d4, _s=s, _y=ybuf:
                                  e.scalar_tensor_tensor(
                                      _y[:, _d * 512:(_d + 1) * 512], Pk[_d][:],
                                      sc(S_CO), x2_f[_s][:, _d * 512:(_d + 1) * 512],
                                      ALU.mult, ALU.add), waits=w,
                                  inc=VEC if d4 == 3 else None)
                att_gate = lastv2
                y_store = E.em("sync", lambda e, _u=2 * t + s, _y=ybuf: e.dma_start(
                    out=y_h[_u * 128:(_u + 1) * 128, :], in_=_y),
                    waits=[(VEC, lastv2)], inc=SY[s], inc_by=16)
                if t == 0 and s == 0:
                    dbg_dump("y0", ybuf[:], [(VEC, lastv2)])
            x_gate2 = (VEC, att_gate)

        if "oc80" in debug:
            out1 = nc.dram_tensor("dbg_oc80", [128, D], I8, kind="ExternalOutput").ap()
            E.em("sync", lambda e: e.dma_start(out=out1, in_=oc8[0]),
                 waits=[(SQ[0], P0_SQ[0]), (SQ[1], P0_SQ[1])], inc=ST, inc_by=16)
            out2 = nc.dram_tensor("dbg_gc80", [128, NCH * HS], I8, kind="ExternalOutput").ap()
            E.em("sync", lambda e: e.dma_start(out=out2, in_=gc8[0]),
                 waits=[(SQ[0], P0_SQ[0]), (SQ[1], P0_SQ[1])], inc=ST, inc_by=16)
        E.check()
        block = ctx.enter_context(nc.Block())

        @block.sync
        def _(sync):
            for fn in E.q["sync"]:
                fn(sync)
            sync.wait_ge(ST.h, ST.n)
            sync.wait_ge(SY[0].h, SY[0].n)
            sync.wait_ge(SY[1].h, SY[1].n)

        @block.tensor
        def _(tensor):
            for fn in E.q["tensor"]:
                fn(tensor)

        @block.vector
        def _(vector):
            for fn in E.q["vector"]:
                fn(vector)

        @block.scalar
        def _(scalar):
            for fn in E.q["scalar"]:
                fn(scalar)

        @block.gpsimd
        def _(gpsimd):
            for fn in E.q["gpsimd"]:
                fn(gpsimd)

    return nc


def _prep_inputs(inputs):
    a = {k: np.ascontiguousarray(np.asarray(v, dtype=np.float32)) for k, v in inputs.items()}
    com = {
        "wqT": np.ascontiguousarray(a["L0_attn_Wq"].T),
        "wkT": np.ascontiguousarray(a["L0_attn_Wk"].T),
        "wvT": np.ascontiguousarray(a["L0_attn_Wv"].T),
        "woT": np.ascontiguousarray(a["L0_attn_Wo"].T),
        "gwT": np.ascontiguousarray(a["L0_ffn_gate_w"].T),
        "vwT": np.ascontiguousarray(a["L0_ffn_val_w"].T),
        "owT": np.ascontiguousarray(a["L0_ffn_out_w"].T),
        "bq": a["L0_attn_bq"], "bk": a["L0_attn_bk"], "bv": a["L0_attn_bv"],
        "bo": a["L0_attn_bo"], "gb": a["L0_ffn_gate_b"], "vb": a["L0_ffn_val_b"],
        "ob": a["L0_ffn_out_b"],
    }
    return a["L0_attn_x"], com


def kernel(**inputs):
    x, com = _prep_inputs(inputs)
    nc = build()
    in_maps = [dict(com, x=np.ascontiguousarray(x[b])) for b in range(8)]
    res = run_bass_kernel_spmd(nc, in_maps, core_ids=list(range(8)))
    return np.stack([r["y"] for r in res.results], axis=0)
